# version 22 (final)
# baseline (speedup 1.0000x reference)
import numpy as np
import concourse.bass as bass
import concourse.mybir as mybir
from concourse.bass_utils import run_bass_kernel_spmd

F32 = mybir.dt.float32
BF16 = mybir.dt.bfloat16
AF = mybir.ActivationFunctionType
OP = mybir.AluOpType
AX = mybir.AxisListType.X

D, S, DI, DS = 1024, 4096, 2048, 16
NT, TK, OWN0 = 8, 512, 2048
CH, WU, NCH = 32, 48, 64
NE, FQ, NEQ = 8, 1024, 32
MINI = 32


def build_nc():
    import contextlib
    nc = bass.Bass("TRN2", target_bir_lowering=False, debug=False)
    ctx = contextlib.ExitStack()
    nc._ctx = ctx

    def din(name, shape, dt=F32):
        return nc.dram_tensor(name, list(shape), dt, kind="ExternalInput").ap()

    x_in = din("x", (128, NT * 8 * TK))
    w_in = din("w_in", (D, 2 * DI))
    w_out = din("w_out", (DI, D))
    w_xp = din("w_xp", (DI, 32))
    wgT = din("wgT", (D, NE))
    w1T = din("w1T", (NE, D, 4096))
    w2T = din("w2T", (NE, 4096, D))
    a_T = din("a_T", (DS, DS))
    dt16 = din("dt16", (1, DS))
    cw_d = din("cw", (128, 64))
    cb_d = din("cb", (128, 16))
    bxi_d = din("b_in_xi", (128, 16))
    bres_d = din("b_in_res", (128, 16))
    bxp_d = din("b_xp2", (DS, 2))
    bout_d = din("b_out_t", (128, 8))
    b1_d = din("b1_t", (128, 256))
    b2_d = din("b2_row", (1, NE * D))
    gxi_d = din("gxi", (128, NT))
    g16_d = din("g16", (DS, NT))
    colg_d = din("colg", (DS, 1))
    esel_d = din("esel", (DS, DS * 128), BF16)
    id_d = din("id128", (128, 128), BF16)
    onc_d = din("ones_c", (128, 1))
    onr_d = din("ones_r", (1, 128))
    onrb_d = din("ones_rb", (1, 128), BF16)
    idf_d = din("id128f", (128, 128))
    out_d = nc.dram_tensor("out", [OWN0, D], F32, kind="ExternalOutput").ap()

    def sbt(name, shape, dt):
        return ctx.enter_context(nc.sbuf_tensor(name, list(shape), dt))

    AB = sbt("AB", (128, 59392), BF16)
    AF32 = sbt("AF", (128, 11264), F32)
    x2 = sbt("x2", (128, 8 * OWN0), BF16)
    hs = sbt("hs", (DS, OWN0), BF16)
    w_all = sbt("w_all", (128, 128), F32)
    rt_s = sbt("rt_s", (128, 16), F32)
    rt_a = sbt("rt_a", (128, 8), F32)
    rt_b = sbt("rt_b", (128, 8), F32)
    vscr = sbt("vscr", (1, 8), F32)
    c_aT = sbt("c_aT", (DS, DS), F32)
    c_dt = sbt("c_dt", (1, DS), F32)
    c_cw = sbt("c_cw", (128, 64), F32)
    c_cbh = sbt("c_cbh", (128, 16), F32)
    c_bxi = sbt("c_bxi", (128, 16), F32)
    c_bxig = sbt("c_bxig", (128, 128), F32)
    c_bres = sbt("c_bres", (128, 16), F32)
    c_bxp = sbt("c_bxp", (DS, 2), F32)
    c_bxpg = sbt("c_bxpg", (DS, 16), F32)
    c_bout = sbt("c_bout", (128, 8), F32)
    c_b1 = sbt("c_b1", (128, 256), F32)
    c_gxi = sbt("c_gxi", (128, NT), F32)
    c_g16 = sbt("c_g16", (DS, NT), F32)
    c_colg = sbt("c_colg", (DS, 1), F32)
    c_esel = sbt("c_esel", (DS, DS * 128), BF16)
    c_id = sbt("c_id", (128, 128), BF16)
    c_onc = sbt("c_onc", (128, 1), F32)
    c_onr = sbt("c_onr", (1, 128), F32)
    c_onrb = sbt("c_onrb", (1, 128), BF16)
    c_idf = sbt("c_idf", (128, 128), F32)
    c_wg = sbt("c_wg", (128, 8 * NE), BF16)
    c_wxp = sbt("c_wxp", (128, 16 * 32), BF16)
    ascan = AF32[:, 0:512]
    ascm = AF32[:, 512:1024]
    dtexp = AF32[:, 1024:1280]
    Hp = AF32[:, 1280:1408]
    Gs = AF32[:, 1408:1472]
    Vs = AF32[:, 1472:1536]
    Bmf = AF32[:, 1536:1568]
    Cmf = AF32[:, 1568:1600]
    BmfB = AF32[:, 1600:1632]
    CmfB = AF32[:, 1632:1664]
    win = AB[:, 0:16384]
    xbf = AB[:, 16384:24576]
    xib = AB[:, 24576:32832]
    xab = AB[:, 32832:41024]
    cvb = AB[:, 41024:43072]
    xcb = AB[:, 41024:42048]
    thb = AB[:, 42048:43072]
    Bm = AB[:, 43072:47168]
    Cm = AB[:, 47168:51264]
    wout = AB[:, 16384:32768]
    ytl = AB[:, 32768:40960]
    xbf3 = AB[:, 51264:55360]
    w1q = AB[:, 0:16384]
    w2q = AB[:, 16384:32768]
    hb4 = AB[:, 32768:40960]
    Yb = AB[:, 40960:57344]
    b2e = AB[:, 57344:59392]
    lnin = AF32[:, 0:4096]
    x_own = AF32[:, 4096:8192]
    sqb = AF32[:, 8192:9216]
    r_mean = AF32[:, 9216:9728]
    r_rstd = AF32[:, 9728:10240]
    r_t1 = AF32[:, 10240:10752]
    r_t2 = AF32[:, 10752:11264]
    ln2 = AF32[:, 0:1024]
    sq2 = AF32[:, 1024:2048]
    outt = AF32[:, 2048:4096]
    st2 = AF32[:, 4112:4128]

    PB = [ctx.enter_context(nc.psum_tensor(f"pb{i}", [128, 512], F32)) for i in range(8)]

    class Sem:
        __slots__ = ("h", "n")

        def __init__(self, name):
            self.h = ctx.enter_context(nc.semaphore(name))
            self.n = 0

    SN = ("sd", "gc", "wxi", "xt0", "xt1", "wr", "wo", "x3", "eqd0", "eqd1",
          "b2d0", "b2d1", "xo", "od0", "od1",
          "scv", "sc2", "sca", "stv", "pA", "mmA", "cv", "th", "xa",
          "smm", "sst", "shr", "hsd", "sqv", "sqT", "mmB", "sv", "x2d",
          "nh", "thr", "rw", "mmY", "yp", "ln2v", "vch", "sch", "wx2", "gt4")
    sm = {n: Sem("s_" + n) for n in SN}

    Q = {"tensor": [], "vector": [], "scalar": [], "sync": [], "gpsimd": []}

    def em(eng, fn):
        Q[eng].append(fn)

    _last_inc = {e: (-1, None, 0) for e in Q}

    def inc_last(eng, s, k=1):
        idx = len(Q[eng]) - 1
        assert _last_inc[eng][0] != idx, f"double inc on {eng} @{idx}"
        s.n += k
        f = Q[eng][idx]
        Q[eng][idx] = (lambda f=f, h=s.h, k=k: (lambda e: f(e).then_inc(h, k)))()
        _last_inc[eng] = (idx, s, s.n)

    def wt(eng, s, v):
        if v > 0:
            em(eng, lambda e, h=s.h, v=v: e.wait_ge(h, v))

    def bar(eng):
        idx, s, v = _last_inc[eng]
        if s is not None and idx == len(Q[eng]) - 1:
            wt(eng, s, v)
        else:
            c = sm["vch" if eng == "vector" else "sch"]
            if eng == "vector":
                em(eng, lambda e: e.memset(vscr[0:1, 0:2], 0.0))
            else:
                em(eng, lambda e: e.copy(out=vscr[0:1, 4:5], in_=c_onc[0:1, 0:1]))
            inc_last(eng, c, 1)
            wt(eng, c, c.n)

    def vbar():
        bar("vector")

    def dma(eng, dst, src, s):
        em(eng, lambda e, d=dst, sr=src: e.dma_start(out=d, in_=sr))
        inc_last(eng, s, 16)

    sd = sm["sd"]
    th0 = sm["th"]
    for dst, src in ((c_aT, a_T), (c_dt, dt16), (c_onr, onr_d)):
        dma("sync", dst[:, :], src[:, :], th0)
    for dst, src in ((c_bxi, bxi_d), (c_gxi, gxi_d), (c_bxp, bxp_d),
                     (c_g16, g16_d), (c_onc, onc_d)):
        dma("sync", dst[:, :], src[:, :], sm["gt4"])
    for i2, (dst, src) in enumerate((
            (c_cw, cw_d), (c_bres, bres_d), (c_bout, bout_d),
            (c_b1, b1_d),
            (c_colg, colg_d), (c_esel, esel_d), (c_id, id_d),
            (c_onrb, onrb_d), (c_idf, idf_d),
            (c_cbh, cb_d))):
        dma("sync" if i2 % 2 == 0 else "scalar", dst[:, :], src[:, :], sd)
    gc = sm["gc"]
    n_sd = sd.n

    scv, sc2, sca = sm["scv"], sm["sc2"], sm["sca"]
    stv = sm["stv"]
    wt("scalar", th0, 48)
    em("scalar", lambda e: e.activation(out=c_dt[:, :], in_=c_dt[:, :], func=AF.Exp))
    inc_last("scalar", scv, 1)
    wt("vector", th0, 48)
    wt("vector", scv, 1)
    em("vector", lambda e: e.tensor_copy(
        out=dtexp[0:1, 0:256].rearrange("p (i j) -> p i j", i=16),
        in_=c_dt[0:1, :].rearrange("p (i o) -> p i o", o=1).broadcast_to([1, DS, DS])))
    em("vector", lambda e: e.tensor_copy(
        out=ascm[0:DS, 0:256].rearrange("p (i j) -> p i j", i=16),
        in_=c_aT[0:DS, :].rearrange("p (o j) -> p o j", o=1).broadcast_to([DS, 16, DS])))
    inc_last("vector", scv, 1)
    wt("tensor", scv, 2)
    em("tensor", lambda e: e.matmul(PB[6][0:DS, 0:256], c_onr[0:1, 0:DS],
                                    dtexp[0:1, 0:256], start=True, stop=True))
    inc_last("tensor", sc2, 1)
    wt("vector", sc2, 1)
    em("vector", lambda e: e.tensor_tensor(out=ascm[0:DS, 0:256], in0=ascm[0:DS, 0:256],
                                           in1=PB[6][0:DS, 0:256], op=OP.mult))
    inc_last("vector", scv, 1)
    wt("scalar", scv, 3)
    em("scalar", lambda e: e.activation(out=ascan[0:DS, 0:256], in_=ascm[0:DS, 0:256],
                                        func=AF.Exp))
    inc_last("scalar", sca, 1)
    em("scalar", lambda e: e.memzero(Hp[0:DS, 0:1]))
    inc_last("scalar", sm["shr"], 1)

    wt("vector", sm["gt4"], 80)
    em("vector", lambda e: e.tensor_tensor(
        out=c_bxig[:, :].rearrange("p (f t) -> p f t", f=16),
        in0=c_bxi[:, :].rearrange("p (f o) -> p f o", o=1).broadcast_to([128, 16, 8]),
        in1=c_gxi[:, :].rearrange("p (o t) -> p o t", o=1).broadcast_to([128, 16, 8]),
        op=OP.mult))
    em("vector", lambda e: e.tensor_tensor(
        out=c_bxpg[0:DS, :].rearrange("p (c t) -> p c t", c=2),
        in0=c_bxp[0:DS, :].rearrange("p (c o) -> p c o", o=1).broadcast_to([DS, 2, 8]),
        in1=c_g16[0:DS, :].rearrange("p (o t) -> p o t", o=1).broadcast_to([DS, 2, 8]),
        op=OP.mult))
    inc_last("vector", stv, 1)
    wt("vector", sd, n_sd)

    wxi = sm["wxi"]
    xtp0 = sm["xt0"]
    dma("gpsimd", xbf[:, 0:S], x_in[:, 0:S], xtp0)
    for k in range(8):
        dma("gpsimd", win[:, k * DI:(k + 1) * DI], w_in[k * 128:(k + 1) * 128, 0:DI],
            wxi if k < 4 else sm["wx2"])
    n_wxi2 = sm["wx2"].n
    dma("gpsimd", c_wg[:, :].rearrange("p (k e) -> p k e", k=8),
        wgT.rearrange("(k p) e -> p k e", p=128), gc)
    dma("gpsimd", c_wxp[:, :].rearrange("p (k c) -> p k c", k=16),
        w_xp.rearrange("(k p) c -> p k c", p=128), gc)

    pA, mmA = sm["pA"], sm["mmA"]
    cv, th_s, xa_s = sm["cv"], sm["th"], sm["xa"]
    xtp = (sm["xt0"], sm["xt1"])
    gA = [0]
    pA_res_done = {}
    pA_wout = {}
    pA_ge = {}

    pool_banks = [[0, 1]]
    pool_pos = [0]
    last_ev = {}

    def mm_group(n_mm, make_mm, evac_eng, make_evac, extra_T_waits=()):
        bk = pool_banks[0][pool_pos[0] % len(pool_banks[0])]
        pool_pos[0] += 1
        pb = PB[bk]
        for s, v in extra_T_waits:
            wt("tensor", s, v)
        wt("tensor", pA, last_ev.get(bk, 0))
        for i in range(n_mm):
            em("tensor", make_mm(i, pb))
        inc_last("tensor", mmA, 1)
        wt(evac_eng, mmA, mmA.n)
        em(evac_eng, make_evac(pb))
        inc_last(evac_eng, pA, 1)
        last_ev[bk] = pA.n
        gA[0] += 1
        return pA.n

    pool_banks[0] = [0, 1, 2, 3, 4, 5]
    vch = sm["vch"]

    smm, sst, shr, hsd = sm["smm"], sm["sst"], sm["shr"], sm["hsd"]

    def scan_step(pos, ncols, src_off, dst_off, hs_col, first_T=(), first_S=()):
        i16 = pos % 16
        t_idx = smm.n
        for s, v in first_T:
            wt("tensor", s, v)
        wt("tensor", shr, shr.n)
        ub = PB[6 + (t_idx % 2)]
        em("tensor", lambda e, i16=i16, so=src_off, n_=ncols, ub=ub: e.matmul(
            ub[0:DS, 0:n_], ascan[0:DS, i16 * 16:(i16 + 1) * 16],
            Hp[0:DS, so:so + n_], start=True, stop=True))
        inc_last("tensor", smm, 1)
        for s, v in first_S:
            wt("scalar", s, v)
        if ncols == 1:
            if pos < OWN0:
                bsel = Bmf[0:DS, pos:pos + 1]
                csel = Cmf[0:DS, pos:pos + 1]
            else:
                bsel = BmfB[0:DS, pos - OWN0:pos - OWN0 + 1]
                csel = CmfB[0:DS, pos - OWN0:pos - OWN0 + 1]
            wt("scalar", shr, shr.n)
            em("scalar", lambda e, b=bsel, c=csel, so=src_off: e.activation(
                out=Vs[0:DS, 0:1], in_=Hp[0:DS, so:so + 1], func=AF.Identity,
                scale=b, bias=c))
            inc_last("scalar", sm["sch"], 1)
            wt("scalar", smm, smm.n)
            em("scalar", lambda e, ub=ub: e.activation(
                out=Gs[0:DS, 0:1], in_=ub[0:DS, 0:1], func=AF.Tanh))
            inc_last("scalar", sm["sch"], 1)
            wt("scalar", sm["sch"], sm["sch"].n)
            em("scalar", lambda e, do=dst_off: e.activation(
                out=Hp[0:DS, do:do + 1], in_=Gs[0:DS, 0:1], func=AF.Identity,
                bias=Vs[0:DS, 0:1]))
            inc_last("scalar", shr, 1)
            return
        wt("scalar", smm, smm.n)
        em("scalar", lambda e, ub=ub, n_=ncols: e.activation(
            out=Gs[0:DS, 0:n_], in_=ub[0:DS, 0:n_], func=AF.Tanh))
        inc_last("scalar", sst, 1)
        bsel = Bm[0:DS, pos: pos + CH * (ncols - 1) + 1: CH]
        csel = Cm[0:DS, pos: pos + CH * (ncols - 1) + 1: CH]
        wt("vector", shr, shr.n)
        em("vector", lambda e, b=bsel, so=src_off, n_=ncols: e.tensor_tensor(
            out=Vs[0:DS, 0:n_], in0=b, in1=Hp[0:DS, so:so + n_], op=OP.mult))
        vbar()
        em("vector", lambda e, c=csel, n_=ncols: e.tensor_tensor(
            out=Vs[0:DS, 0:n_], in0=Vs[0:DS, 0:n_], in1=c, op=OP.add))
        vbar()
        wt("vector", sst, sst.n)
        em("vector", lambda e, do=dst_off, n_=ncols: e.tensor_tensor(
            out=Hp[0:DS, do:do + n_], in0=Gs[0:DS, 0:n_], in1=Vs[0:DS, 0:n_],
            op=OP.add))
        inc_last("vector", shr, 1)
        wt("vector", shr, shr.n)
        if hs_col is not None:
            em("vector", lambda e, do=dst_off, a=hs_col: e.tensor_copy(
                out=hs[0:DS, a: a + CH * (NCH - 1) + 1: CH], in_=Hp[0:DS, do:do + NCH]))

    pA_xi = {}
    pA_xd = {}
    mmA_xi = {}
    mmA_xd = {}
    halo_done = {}

    xdbl_i = {}

    def emit_xdbl(td):
        bkA = pool_banks[0][pool_pos[0] % len(pool_banks[0])]
        pool_pos[0] += 1
        bkB = pool_banks[0][pool_pos[0] % len(pool_banks[0])]
        pool_pos[0] += 1
        pbA, pbB = PB[bkA], PB[bkB]
        if td == 0:
            wt("tensor", gc, 32)
        wt("tensor", pA, last_ev.get(bkA, 0))
        wt("tensor", pA, last_ev.get(bkB, 0))
        iv = []
        for i in range(16):
            def fmm(e, i=i, td=td, pbA=pbA, pbB=pbB):
                e.wait_ge(xa_s.h, td * 8 + i // 2 + 1)
                e.matmul(pbA[0:DS, :], c_wxp[:, i * 32: i * 32 + DS],
                         xab[:, i * TK:(i + 1) * TK], start=(i == 0), stop=(i == 15))
                return e.matmul(pbB[0:DS, :], c_wxp[:, i * 32 + DS: i * 32 + 2 * DS],
                                xab[:, i * TK:(i + 1) * TK], start=(i == 0),
                                stop=(i == 15))
            em("tensor", fmm)
            inc_last("tensor", mmA, 1)
            iv.append(mmA.n)
        xdbl_i[td] = iv
        mmA_xd[td] = mmA.n
        for part, (pb, bk) in enumerate(((pbA, bkA), (pbB, bkB))):
            wt("scalar", mmA, mmA_xd[td])
            em("scalar", lambda e, pb=pb, part=part, td=td: e.activation(
                out=(Bm if part == 0 else Cm)[0:DS, td * TK:(td + 1) * TK],
                in_=pb[0:DS, :], func=AF.Identity, scale=c_g16[:, td:td + 1],
                bias=c_bxpg[:, part * 8 + td: part * 8 + td + 1]))
            inc_last("scalar", pA, 1)
            last_ev[bk] = pA.n
        pA_xd[td] = pA.n

    for t in range(NT):
        pr = t % 2
        if t >= 2:
            wt("gpsimd", mmA, mmA_xi[t - 2])
        if t >= 1:
            dma("gpsimd", xbf[:, pr * S:(pr + 1) * S], x_in[:, t * S:(t + 1) * S], xtp[pr])
        first = [(xtp[pr], 16 * (t // 2 + 1))]
        if t == 0:
            first += [(stv, 1)]
        if t >= 1:
            wt("scalar", sm["sch"], halo_done[t - 1])

        def emit_conv4(f4, t=t):
            xoM = xab[:, f4 * 4 * TK: (f4 + 1) * 4 * TK]
            wt("vector", cv, cv.n)
            if t >= 1:
                wt("vector", mmA, xdbl_i[t - 1][f4 * 4 + 3])
            wt("vector", pA, pA_xi[(t, f4 * 4 + 3)])

            def muls(tap, dst4, f4=f4, bias=False):
                for j in range(4):
                    f = f4 * 4 + j
                    if bias:
                        em("vector", lambda e, f=f, tap=tap, j=j, dst4=dst4:
                           e.tensor_scalar(
                               out=dst4[:, j * TK:(j + 1) * TK],
                               in0=xib[:, f * 516 + tap + 1: f * 516 + tap + 513],
                               scalar1=c_cw[:, f * 4 + tap: f * 4 + tap + 1],
                               scalar2=c_cbh[:, f:f + 1],
                               op0=OP.mult, op1=OP.add))
                    else:
                        em("vector", lambda e, f=f, tap=tap, j=j, dst4=dst4:
                           e.tensor_scalar(
                               out=dst4[:, j * TK:(j + 1) * TK],
                               in0=xib[:, f * 516 + tap + 1: f * 516 + tap + 513],
                               scalar1=c_cw[:, f * 4 + tap: f * 4 + tap + 1],
                               scalar2=None, op0=OP.mult))
            muls(0, xoM, bias=True)
            muls(1, cvb)
            inc_last("vector", vch, 1)
            for tap in (2, 3):
                wt("vector", vch, vch.n)
                em("vector", lambda e, xoM=xoM: e.tensor_tensor(
                    out=xoM, in0=xoM, in1=cvb, op=OP.add))
                inc_last("vector", vch, 1)
                wt("vector", vch, vch.n)
                muls(tap, cvb)
                inc_last("vector", vch, 1)
            wt("vector", vch, vch.n)
            em("vector", lambda e, xoM=xoM: e.tensor_tensor(
                out=xoM, in0=xoM, in1=cvb, op=OP.add))
            inc_last("vector", cv, 1)

        def emit_silu(ts_, fp):
            xo_ = xab[:, fp * 2 * TK:(fp + 1) * 2 * TK]
            bar("scalar")
            wt("scalar", cv, ts_ * 4 + fp // 2 + 1)
            em("scalar", lambda e, xo_=xo_: e.activation(
                out=xo_, in_=xo_, func=AF.Silu))
            inc_last("scalar", xa_s, 1)

        for f in range(16):
            def mk_mm(i, pb, f=f, pr=pr, t=t):
                def g(e, i=i, pb=pb, f=f, pr=pr, t=t):
                    if t == 0 and f == 0 and i == 0:
                        e.wait_ge(wxi.h, 64)
                    if t == 0 and f == 0 and i == 4:
                        e.wait_ge(sm["wx2"].h, n_wxi2)
                    return e.matmul(
                        pb[:, :], win[:, i * DI + f * 128: i * DI + (f + 1) * 128],
                        xbf[:, pr * S + i * TK: pr * S + (i + 1) * TK],
                        start=(i == 0), stop=(i == 7))
                return g

            def mk_ev(pb, f=f, t=t):
                return lambda e: e.activation(
                    out=xib[:, f * 516 + 4: f * 516 + 516], in_=pb[:, :],
                    func=AF.Identity, scale=c_gxi[:, t:t + 1],
                    bias=c_bxig[:, f * 8 + t: f * 8 + t + 1])
            pA_xi[(t, f)] = mm_group(8, mk_mm, "scalar", mk_ev, extra_T_waits=first)
            first = []
            if t in (2, 3, 4, 5):
                if f % 2 == 0:
                    i = (t - 2) * 8 + f // 2
                    fT, fS = ([], [])
                    if i == 0:
                        fT = [(sca, 1)]
                        fS = [(sca, 1), (stv, 2)]
                    scan_step(i, 1, i % 2, 1 - i % 2, None, first_T=fT, first_S=fS)
            elif t in (6, 7):
                i = (t - 6) * 16 + f
                fT = [(pA, pA_xd[4])] if i == 0 else []
                fS = [(stv, 3)] if i == 0 else []
                scan_step(OWN0 + i, 1, i % 2, 1 - i % 2, None, first_T=fT, first_S=fS)
            if f in (1, 2) and t >= 1:
                emit_silu(t - 1, 5 + f)
            if f == 3 and t >= 1:
                emit_xdbl(t - 1)
            if f == 4 and t in (1, 5):
                bf_, cf_, off = (Bmf, Cmf, 0) if t == 1 else (BmfB, CmfB, OWN0)
                bar("scalar")
                wt("scalar", pA, pA_xd[t - 1])
                em("scalar", lambda e, bf_=bf_, off=off: e.copy(
                    out=bf_[0:DS, :], in_=Bm[0:DS, off:off + MINI]))
                em("scalar", lambda e, cf_=cf_, off=off: e.copy(
                    out=cf_[0:DS, :], in_=Cm[0:DS, off:off + MINI]))
                inc_last("scalar", stv, 1)
        mmA_xi[t] = mmA.n
        for f4 in range(4):
            emit_conv4(f4)
            if f4 < 3:
                emit_silu(t, f4 * 2)
                emit_silu(t, f4 * 2 + 1)
        if t < NT - 1:
            wt("scalar", cv, t * 4 + 4)
            em("scalar", lambda e: e.copy(
                out=AB[:, 24577:24577 + 16 * 516].rearrange(
                    "p (b c) -> p b c", b=16)[:, :, 0:3],
                in_=AB[:, 25089:25089 + 16 * 516].rearrange(
                    "p (b c) -> p b c", b=16)[:, :, 0:3]))
            inc_last("scalar", sm["sch"], 1)
            halo_done[t] = sm["sch"].n
    emit_silu(NT - 1, 6)
    emit_silu(NT - 1, 7)
    emit_xdbl(NT - 1)

    _pre = []
    for f in range(16):
        _pre.append(lambda e, f=f: e.memzero(xib[:, f * 516: f * 516 + 4]))
    Q["scalar"] = _pre + Q["scalar"]
    i0, s0_, v0 = _last_inc["scalar"]
    _last_inc["scalar"] = (i0 + len(_pre) if i0 >= 0 else -1, s0_, v0)

    wt("vector", pA, pA_xd[NT - 1])
    wt("vector", shr, shr.n)
    vbar()
    em("vector", lambda e: e.tensor_copy(
        out=Hp[0:DS, 0:NCH].rearrange("p (o c) -> p o c", o=1),
        in_=Hp[0:DS, 0:1].rearrange("p (o c) -> p o c", o=1).broadcast_to([DS, 1, NCH])))
    inc_last("vector", shr, 1)
    vbar()
    em("vector", lambda e: e.tensor_scalar_mul(Hp[0:DS, 0:1], Hp[0:DS, 0:1],
                                               c_colg[0:DS, 0:1]))
    inc_last("vector", shr, 1)
    vbar()
    fT = [(pA, pA_xd[NT - 1])]
    for t in range(WU + CH):
        pos = OWN0 - WU + t
        scan_step(pos, NCH, (t % 2) * NCH, (1 - t % 2) * NCH,
                  (t - WU) if t >= WU else None, first_T=fT)
        fT = []
    inc_last("vector", hsd, 1)

    wr, wo, x3, xo = sm["wr"], sm["wo"], sm["x3"], sm["xo"]
    sqv, sqT, mmB, sv = sm["sqv"], sm["sqT"], sm["mmB"], sm["sv"]
    x2d, nh_s, thr, rw = sm["x2d"], sm["nh"], sm["thr"], sm["rw"]

    dma("gpsimd", xbf3[:, :], x_in[:, 4 * S:5 * S], x3)
    wt("gpsimd", mmA, mmA_xi[NT - 1])
    for k in range(8):
        dma("gpsimd", win[:, k * DI:(k + 1) * DI], w_in[k * 128:(k + 1) * 128, DI:2 * DI], wr)
    wt("gpsimd", cv, 32)
    wt("gpsimd", pA, 144)
    for k in range(16):
        dma("gpsimd", wout[:, k * D:(k + 1) * D], w_out[k * 128:(k + 1) * 128, :], wo)

    mmB_rt3 = {}
    mmA_res = {}
    mmA_wout = {}

    def emit_router(tt):
        for sub in range(4):
            bank = 6 + sub % 2
            pb = PB[bank]
            gsub = tt * 4 + sub
            wt("tensor", x2d, 8 * (tt + 1))
            if sub >= 2:
                wt("tensor", rw, 4 * tt + sub - 1)
            for i in range(8):
                em("tensor", lambda e, i=i, pb=pb, tt=tt, sub=sub: e.matmul(
                    pb[:, 0:8],
                    x2[:, i * OWN0 + tt * TK + sub * 128: i * OWN0 + tt * TK + (sub + 1) * 128],
                    c_wg[:, i * 8:(i + 1) * 8], start=(i == 0), stop=(i == 7)))
            inc_last("tensor", mmB, 1)
            wt("vector", mmB, mmB.n)
            vbar()
            em("vector", lambda e, pb=pb: e.reduce_max(rt_s[:, 0:1], pb[:, 0:8], axis=AX))
            vbar()
            em("vector", lambda e: e.tensor_scalar_mul(rt_s[:, 1:2], rt_s[:, 0:1], -0.5))
            inc_last("vector", nh_s, 1)
            bar("scalar")
            wt("scalar", nh_s, nh_s.n)
            wt("scalar", mmB, mmB.n)
            em("scalar", lambda e, pb=pb: e.activation(
                out=rt_s[:, 8:16], in_=pb[:, 0:8], func=AF.Tanh, scale=0.5,
                bias=rt_s[:, 1:2]))
            inc_last("scalar", thr, 1)
            wt("vector", thr, thr.n)
            em("vector", lambda e: e.tensor_scalar(
                out=rt_a[:, :], in0=rt_s[:, 8:16], scalar1=-1.0, scalar2=1.0,
                op0=OP.mult, op1=OP.add))
            vbar()
            em("vector", lambda e: e.reciprocal(out=rt_b[:, :], in_=rt_a[:, :]))
            vbar()
            em("vector", lambda e: e.scalar_tensor_tensor(
                out=rt_a[:, :], in0=rt_s[:, 8:16], scalar=1.0, in1=rt_b[:, :],
                op0=OP.add, op1=OP.mult))
            vbar()
            em("vector", lambda e: e.reduce_sum(rt_s[:, 2:3], rt_a[:, :], axis=AX))
            vbar()
            em("vector", lambda e: e.reciprocal(out=rt_s[:, 3:4], in_=rt_s[:, 2:3]))
            vbar()
            em("vector", lambda e, pb=pb: e.tensor_scalar(
                out=rt_b[:, :], in0=pb[:, 0:8], scalar1=rt_s[:, 0:1], scalar2=None,
                op0=OP.is_ge))
            vbar()
            em("vector", lambda e, gsub=gsub: e.tensor_scalar_mul(
                w_all[:, gsub * 8:(gsub + 1) * 8], rt_b[:, :], rt_s[:, 3:4]))
            inc_last("vector", rw, 1)
        mmB_rt3[tt] = mmB.n

    for tt in range(4):
        if tt > 0:
            wt("gpsimd", mmA, mmA_res[tt - 1])
            dma("gpsimd", xbf3[:, :], x_in[:, (4 + tt) * S:(5 + tt) * S], x3)
        if tt > 0:
            wt("sync", pA, pA_wout[(tt - 1, 7)])
        dma("sync", x_own[:, :], x_in[:, (4 + tt) * S:(5 + tt) * S], xo)
        for f in range(16):
            ew = []
            if f == 0:
                ew = [(hsd, 1), (wr, 16 * 8), (x3, 16 * (tt + 1))]

            def mk_mm(i, pb, f=f, tt=tt):
                if i < 8:
                    return lambda e: e.matmul(
                        pb[:, :], win[:, i * DI + f * 128: i * DI + (f + 1) * 128],
                        xbf3[:, i * TK:(i + 1) * TK], start=(i == 0), stop=False)
                return lambda e: e.matmul(
                    pb[:, :], c_esel[0:DS, f * 128:(f + 1) * 128],
                    hs[0:DS, tt * TK:(tt + 1) * TK], start=False, stop=True)

            def mk_ev(pb, f=f):
                return lambda e: e.activation(
                    out=ytl[:, f * TK:(f + 1) * TK], in_=pb[:, :],
                    func=AF.Identity, bias=c_bres[:, f:f + 1])
            mm_group(9, mk_mm, "scalar", mk_ev, extra_T_waits=ew)
        pA_res_done[tt] = pA.n
        if tt > 0:
            emit_router(tt - 1)
        mmA_res[tt] = mmA.n
        for do in range(8):
            ew = [(pA, pA_res_done[tt])] if do == 0 else []
            if do == 0 and tt == 0:
                ew.append((wo, 256))

            def mk_mm(i, pb, do=do):
                return lambda e: e.matmul(
                    pb[:, :], wout[:, i * D + do * 128: i * D + (do + 1) * 128],
                    ytl[:, i * TK:(i + 1) * TK], start=(i == 0), stop=(i == 15))

            def mk_ev(pb, do=do):
                return lambda e: e.scalar_tensor_tensor(
                    out=lnin[:, do * TK:(do + 1) * TK], in0=pb[:, :],
                    scalar=c_bout[:, do:do + 1], in1=x_own[:, do * TK:(do + 1) * TK],
                    op0=OP.add, op1=OP.add)
            if do == 0:
                wt("vector", xo, 16 * (tt + 1))
            pA_wout[(tt, do)] = mm_group(16, mk_mm, "vector", mk_ev, extra_T_waits=ew)
            bar("scalar")
            if do >= 2:
                wt("scalar", sqT, 8 * tt + do - 1)
            wt("scalar", pA, pA_wout[(tt, do)])
            em("scalar", lambda e, do=do: e.activation(
                out=sqb[:, (do % 2) * TK:(do % 2 + 1) * TK],
                in_=lnin[:, do * TK:(do + 1) * TK], func=AF.Square))
            inc_last("scalar", sqv, 1)
            if do == 0:
                wt("tensor", rw, 4 * tt)
            wt("tensor", pA, pA_wout[(tt, do)])
            em("tensor", lambda e, do=do: e.matmul(
                PB[6][0:1, :], c_onc[:, 0:1], lnin[:, do * TK:(do + 1) * TK],
                start=(do == 0), stop=(do == 7)))
            if do == 7:
                inc_last("tensor", mmB, 1)
            wt("tensor", sqv, 8 * tt + do + 1)
            em("tensor", lambda e, do=do: e.matmul(
                PB[7][0:1, :], c_onc[:, 0:1], sqb[:, (do % 2) * TK:(do % 2 + 1) * TK],
                start=(do == 0), stop=(do == 7)))
            inc_last("tensor", sqT, 1)
        mmA_wout[tt] = mmA.n
        wt("vector", mmB, mmB.n)
        wt("vector", sqT, 8 * (tt + 1))
        em("vector", lambda e: e.tensor_scalar_mul(r_mean[0:1, :], PB[6][0:1, :], 1.0 / D))
        em("vector", lambda e: e.tensor_scalar_mul(r_t1[0:1, :], PB[7][0:1, :], 1.0 / D))
        inc_last("vector", sv, 1)
        wt("vector", sv, sv.n)
        em("vector", lambda e: e.tensor_tensor(out=r_t2[0:1, :], in0=r_mean[0:1, :],
                                               in1=r_mean[0:1, :], op=OP.mult))
        vbar()
        em("vector", lambda e: e.tensor_tensor(out=r_t1[0:1, :], in0=r_t1[0:1, :],
                                               in1=r_t2[0:1, :], op=OP.subtract))
        vbar()
        em("vector", lambda e: e.tensor_scalar_add(r_t1[0:1, :], r_t1[0:1, :], 1e-5))
        inc_last("vector", sv, 1)
        wt("tensor", sv, sv.n)
        for c in range(4):
            em("tensor", lambda e, c=c: e.transpose(
                PB[6][:, c:c + 1], r_t1[0:1, c * 128:(c + 1) * 128], c_onr[0:1, 0:1]))
        inc_last("tensor", mmB, 1)
        wt("vector", mmB, mmB.n)
        em("vector", lambda e: e.tensor_copy(out=r_t2[:, 0:4], in_=PB[6][:, 0:4]))
        vbar()
        em("vector", lambda e: e.reciprocal(out=r_t2[:, 4:8], in_=r_t2[:, 0:4]))
        vbar()
        em("vector", lambda e: e.tensor_scalar(out=r_t2[:, 8:12], in0=r_t2[:, 4:8],
                                               scalar1=0.5, scalar2=0.5,
                                               op0=OP.mult, op1=OP.add))
        for _ in range(6):
            vbar()
            em("vector", lambda e: e.reciprocal(out=r_t2[:, 12:16], in_=r_t2[:, 8:12]))
            vbar()
            em("vector", lambda e: e.scalar_tensor_tensor(
                out=r_t2[:, 12:16], in0=r_t2[:, 12:16], scalar=0.5, in1=r_t2[:, 4:8],
                op0=OP.mult, op1=OP.mult))
            vbar()
            em("vector", lambda e: e.scalar_tensor_tensor(
                out=r_t2[:, 8:12], in0=r_t2[:, 8:12], scalar=0.5, in1=r_t2[:, 12:16],
                op0=OP.mult, op1=OP.add))
        inc_last("vector", sv, 1)
        wt("tensor", sv, sv.n)
        for c in range(4):
            em("tensor", lambda e, c=c: e.transpose(
                PB[7][0:1, c * 128:(c + 1) * 128], r_t2[:, 8 + c:9 + c], c_idf[:, :]))
        inc_last("tensor", mmB, 1)
        wt("vector", mmB, mmB.n)
        em("vector", lambda e: e.tensor_copy(out=r_rstd[0:1, :], in_=PB[7][0:1, 0:512]))
        inc_last("vector", sv, 1)
        wt("tensor", sv, sv.n)
        em("tensor", lambda e: e.matmul(PB[6][:, :], c_onr[0:1, :], r_mean[0:1, :],
                                        start=True, stop=True))
        inc_last("tensor", mmB, 1)
        em("tensor", lambda e: e.matmul(PB[7][:, :], c_onr[0:1, :], r_rstd[0:1, :],
                                        start=True, stop=True))
        inc_last("tensor", mmB, 1)
        wt("vector", mmB, mmB.n)
        for k in range(8):
            vbar()
            em("vector", lambda e, k=k: e.tensor_tensor(
                out=sqb[:, 0:TK], in0=lnin[:, k * TK:(k + 1) * TK], in1=PB[6][:, :],
                op=OP.subtract))
            vbar()
            em("vector", lambda e, k=k, tt=tt: e.tensor_tensor(
                out=x2[:, k * OWN0 + tt * TK: k * OWN0 + (tt + 1) * TK],
                in0=sqb[:, 0:TK], in1=PB[7][:, :], op=OP.mult))
            inc_last("vector", x2d, 1)

    eqdp = (sm["eqd0"], sm["eqd1"])
    b2dp = (sm["b2d0"], sm["b2d1"])
    mmY, yp = sm["mmY"], sm["yp"]
    pool_banks[0] = [0, 1]
    wt("vector", rw, rw.n)
    yg = [0]
    def emit_p4_dmas(eq):
        e_, q_ = eq // 4, eq % 4
        buf = eq % 2
        if eq == 0:
            wt("gpsimd", mmA, mmA_res[3])
        elif eq >= 2:
            wt("gpsimd", mmY, 32 * (eq - 1))
        for k in range(8):
            dma("gpsimd", w1q[:, buf * 8192 + k * FQ: buf * 8192 + (k + 1) * FQ],
                w1T[e_, k * 128:(k + 1) * 128, q_ * FQ:(q_ + 1) * FQ], eqdp[buf])
        if eq == 0:
            wt("gpsimd", mmA, mmA_wout[3])
        for f in range(8):
            dma("gpsimd", w2q[:, buf * 8192 + f * FQ: buf * 8192 + (f + 1) * FQ],
                w2T[e_, q_ * FQ + f * 128: q_ * FQ + (f + 1) * 128, :], eqdp[buf])
        if q_ == 0:
            dma("gpsimd", b2e[0:1, (e_ % 2) * D:(e_ % 2 + 1) * D],
                b2_d[0:1, e_ * D:(e_ + 1) * D], b2dp[e_ % 2])

    def emit_mm1(eq, wp):
        buf = eq % 2
        gwp = eq * 4 + wp
        for f in range(8):
            ew = []
            if f == 0:
                if wp == 0:
                    ew.append((eqdp[buf], 256 * (eq // 2 + 1)))
                if eq == 0:
                    ew.append((x2d, 8 * (wp + 1)))

            def mk_mm(i, pb, f=f, wp=wp, buf=buf):
                return lambda e: e.matmul(
                    pb[:, :],
                    w1q[:, buf * 8192 + i * FQ + f * 128: buf * 8192 + i * FQ + (f + 1) * 128],
                    x2[:, i * OWN0 + wp * TK: i * OWN0 + (wp + 1) * TK],
                    start=(i == 0), stop=(i == 7))

            def mk_ev(pb, f=f, wp=wp, eq=eq):
                return lambda e: e.activation(
                    out=hb4[:, (wp % 2) * 4096 + f * TK: (wp % 2) * 4096 + (f + 1) * TK],
                    in_=pb[:, :], func=AF.Gelu,
                    bias=c_b1[:, eq * 8 + f: eq * 8 + f + 1])
            pA_ge[gwp] = mm_group(8, mk_mm, "scalar", mk_ev, extra_T_waits=ew)

    def emit_mm2(eq, wp):
        e_, q_ = eq // 4, eq % 4
        buf = eq % 2
        gwp = eq * 4 + wp
        wt("tensor", pA, pA_ge[gwp])
        for w2i in range(2):
            for sub in range(2):
                for dh in range(2):
                    bank = 2 + (yg[0] % 4)
                    pb = PB[bank]
                    if yg[0] >= 4:
                        if yg[0] % 2 == 0:
                            wt("tensor", yp, yg[0] - 2)
                    else:
                        wt("tensor", pA, last_ev.get(bank, 0))
                    gsub = wp * 4 + w2i * 2 + sub
                    for f in range(8):
                        em("tensor", lambda e, f=f, w2i=w2i, sub=sub, dh=dh, wp=wp,
                           buf=buf, pb=pb, q_=q_: e.matmul(
                               pb[:, :],
                               hb4[:, (wp % 2) * 4096 + f * TK + w2i * 256 + sub * 128:
                                   (wp % 2) * 4096 + f * TK + w2i * 256 + (sub + 1) * 128],
                               w2q[:, buf * 8192 + f * FQ + dh * TK:
                                   buf * 8192 + f * FQ + (dh + 1) * TK],
                               start=(f == 0), stop=(f == 7 and q_ != 3)))
                    if q_ == 3:
                        wt("tensor", b2dp[e_ % 2], 16 * (e_ // 2 + 1))
                        em("tensor", lambda e, e_=e_, dh=dh, pb=pb: e.matmul(
                            pb[:, :], c_onrb[0:1, :],
                            b2e[0:1, (e_ % 2) * D + dh * TK: (e_ % 2) * D + (dh + 1) * TK],
                            start=False, stop=True))
                    inc_last("tensor", mmY, 1)
                    wt("vector", mmY, mmY.n)
                    wcol = gsub * 8 + e_
                    if eq == 0:
                        em("vector", lambda e, pb=pb, gsub=gsub, dh=dh, wcol=wcol: e.tensor_scalar(
                            out=Yb[:, gsub * D + dh * TK: gsub * D + (dh + 1) * TK],
                            in0=pb[:, :], scalar1=w_all[:, wcol:wcol + 1],
                            scalar2=None, op0=OP.mult))
                    else:
                        em("vector", lambda e, pb=pb, gsub=gsub, dh=dh, wcol=wcol: e.scalar_tensor_tensor(
                            out=Yb[:, gsub * D + dh * TK: gsub * D + (dh + 1) * TK],
                            in0=pb[:, :], scalar=w_all[:, wcol:wcol + 1],
                            in1=Yb[:, gsub * D + dh * TK: gsub * D + (dh + 1) * TK],
                            op0=OP.mult, op1=OP.add))
                    inc_last("vector", yp, 1)
                    yg[0] += 1

    odp = (sm["od0"], sm["od1"])
    ln2v = sm["ln2v"]
    sqv2, nrm = sm["sqv"], sm["thr"]
    ln2p = (AF32[:, 0:1024], AF32[:, 1024:2048],
            AF32[:, 4608:5632], AF32[:, 5632:6656])
    sqscr = AF32[:, 6656:7680]
    idscr = AF32[:, 7680:8704]
    st4 = AF32[:, 8704:8832]
    yp31 = {}
    ev_done = {}
    ch_done = {}
    sum_done = {}
    nrm_done = {}

    def emit_ln2_chain(q):
        def st(j):
            return st4[:, q * 32 + j: q * 32 + j + 25: 8]
        wt("vector", sqv2, sum_done[q * 4 + 3])
        em("vector", lambda e: e.tensor_scalar_mul(st(2), st(0), 1.0 / D))
        vbar()
        em("vector", lambda e: e.tensor_tensor(
            out=st(3), in0=st(2), in1=st(2), op=OP.mult))
        vbar()
        em("vector", lambda e: e.scalar_tensor_tensor(
            out=st(3), in0=st(1), scalar=1.0 / D, in1=st(3),
            op0=OP.mult, op1=OP.subtract))
        vbar()
        em("vector", lambda e: e.tensor_scalar_add(st(3), st(3), 1e-5))
        vbar()
        em("vector", lambda e: e.reciprocal(out=st(4), in_=st(3)))
        vbar()
        em("vector", lambda e: e.tensor_scalar(
            out=st(5), in0=st(4), scalar1=0.5, scalar2=0.5,
            op0=OP.mult, op1=OP.add))
        for _ in range(6):
            vbar()
            em("vector", lambda e: e.reciprocal(out=st(6), in_=st(5)))
            vbar()
            em("vector", lambda e: e.scalar_tensor_tensor(
                out=st(6), in0=st(6), scalar=0.5, in1=st(4),
                op0=OP.mult, op1=OP.mult))
            vbar()
            em("vector", lambda e: e.scalar_tensor_tensor(
                out=st(5), in0=st(5), scalar=0.5, in1=st(6),
                op0=OP.mult, op1=OP.add))
        vbar()
        em("vector", lambda e: e.scalar_tensor_tensor(
            out=st(7), in0=st(2), scalar=-1.0, in1=st(5),
            op0=OP.mult, op1=OP.mult))
        inc_last("vector", ln2v, 1)
        ch_done[q] = ln2v.n
        for s2 in range(q * 4, q * 4 + 4):
            p_ = s2 % 2
            bar("scalar")
            wt("scalar", ln2v, ch_done[q])
            if s2 >= 2:
                wt("scalar", odp[p_], 16 * (s2 // 2))
            ob = outt[:, p_ * D:(p_ + 1) * D]
            em("scalar", lambda e, s2=s2, ob=ob: e.activation(
                out=ob, in_=ln2p[s2 % 4], func=AF.Identity,
                scale=st4[:, s2 * 8 + 5: s2 * 8 + 6],
                bias=st4[:, s2 * 8 + 7: s2 * 8 + 8]))
            inc_last("scalar", nrm, 1)
            nrm_done[s2] = nrm.n
            wt("sync", nrm, nrm_done[s2])
            dma("sync", out_d[s2 * 128:(s2 + 1) * 128, :], ob, odp[p_])

    def emit_ln2_sub(s):
        p = s % 4
        lp = ln2p[p]
        tv = {}
        for k in range(8):
            bank = 6 + k // 4
            pb = PB[bank]
            if k % 4 == 0:
                if s == 0 and k == 0:
                    wt("tensor", rw, 16)
                if s >= 1:
                    wt("tensor", ln2v, ev_done[s - 1])
            pbv = pb[:, (k % 4) * 64:(k % 4) * 64 + 64].bitcast(BF16)
            em("tensor", lambda e, k=k, s=s, pbv=pbv: e.transpose(
                pbv, x2[:, k * OWN0 + s * 128: k * OWN0 + (s + 1) * 128],
                c_id[:, :]))
            inc_last("tensor", mmB, 1)
            tv[bank] = mmB.n
        if s % 4 == 0:
            wt("vector", yp, yp31[s // 4])
        if s >= 4:
            wt("vector", nrm, nrm_done[s - 4])
        for half in range(2):
            pb = PB[6 + half]
            wt("vector", mmB, tv[6 + half])
            em("vector", lambda e, pb=pb, s=s, half=half, lp=lp: e.tensor_tensor(
                out=lp[:, half * 512:(half + 1) * 512],
                in0=pb[:, 0:256].bitcast(BF16),
                in1=Yb[:, s * D + half * 512: s * D + (half + 1) * 512], op=OP.add))
            inc_last("vector", ln2v, 1)
        ev_done[s] = ln2v.n
        bar("scalar")
        wt("scalar", ln2v, ev_done[s])
        if s >= 1:
            wt("scalar", sqv2, sum_done[s - 1])
        em("scalar", lambda e, lp=lp, s=s: e.activation(
            out=sqscr, in_=lp, func=AF.Square,
            accum_out=st4[:, s * 8 + 1: s * 8 + 2]))
        inc_last("scalar", sqv2, 1)
        em("scalar", lambda e, lp=lp, s=s: e.activation(
            out=idscr, in_=lp, func=AF.Identity,
            accum_out=st4[:, s * 8: s * 8 + 1]))
        inc_last("scalar", sqv2, 1)
        sum_done[s] = sqv2.n
        if s % 4 == 3:
            emit_ln2_chain(s // 4)

    prev = None
    for eq in range(NEQ):
        for wp in range(4):
            if wp == 0:
                emit_p4_dmas(eq)
            emit_mm1(eq, wp)
            if (eq, wp) == (0, 0):
                emit_router(3)
            if prev is not None:
                emit_mm2(*prev)
                if prev[0] == NEQ - 1:
                    q = prev[1]
                    yp31[q] = yp.n
                    for s in range(q * 4, q * 4 + 4):
                        emit_ln2_sub(s)
            prev = (eq, wp)
    emit_mm2(*prev)
    yp31[3] = yp.n
    for s in range(12, 16):
        emit_ln2_sub(s)
    wt("sync", odp[0], odp[0].n)
    wt("sync", odp[1], odp[1].n)

    with nc.Block() as block:
        @block.tensor
        def _(e):
            for f in Q["tensor"]:
                f(e)

        @block.vector
        def _(e):
            for f in Q["vector"]:
                f(e)

        @block.scalar
        def _(e):
            for f in Q["scalar"]:
                f(e)

        @block.sync
        def _(e):
            for f in Q["sync"]:
                f(e)

        @block.gpsimd
        def _(e):
            for f in Q["gpsimd"]:
                f(e)

    return nc


def _host_inputs(d):
    import ml_dtypes
    bf = ml_dtypes.bfloat16
    f32 = np.float32
    c = {}
    c["w_in"] = np.ascontiguousarray(d["L0_attn_W_in"], f32)
    c["w_out"] = np.ascontiguousarray(d["L0_attn_W_out"], f32)
    c["w_xp"] = np.ascontiguousarray(d["L0_attn_W_xp"], f32)
    c["wgT"] = np.ascontiguousarray(np.asarray(d["L0_ffn_Wg"], f32).T)
    c["w1T"] = np.ascontiguousarray(np.asarray(d["L0_ffn_W1"], f32).transpose(0, 2, 1))
    c["w2T"] = np.ascontiguousarray(np.asarray(d["L0_ffn_W2"], f32).transpose(0, 2, 1))
    c["a_T"] = np.ascontiguousarray(np.asarray(d["L0_attn_A"], f32).T)
    c["dt16"] = np.ascontiguousarray(np.asarray(d["L0_attn_dt_param"], f32)[:16].reshape(1, 16))
    cw = np.ascontiguousarray(np.asarray(d["L0_attn_conv_w"], f32)[:, 0, :])
    c["cw"] = np.ascontiguousarray(cw.reshape(16, 128, 4).transpose(1, 0, 2).reshape(128, 64))
    c["cb"] = np.ascontiguousarray(np.asarray(d["L0_attn_conv_b"], f32).reshape(16, 128).T)
    b_in = np.asarray(d["L0_attn_b_in"], f32)
    c["b_in_xi"] = np.ascontiguousarray(b_in[:DI].reshape(16, 128).T)
    c["b_in_res"] = np.ascontiguousarray(b_in[DI:].reshape(16, 128).T)
    bxp = np.asarray(d["L0_attn_b_xp"], f32)
    c["b_xp2"] = np.ascontiguousarray(np.stack([bxp[:16], bxp[16:]], axis=1))
    c["b_out_t"] = np.ascontiguousarray(np.asarray(d["L0_attn_b_out"], f32).reshape(8, 128).T)
    b1 = np.asarray(d["L0_ffn_b1"], f32)
    c["b1_t"] = np.ascontiguousarray(b1.reshape(8, 32, 128).transpose(2, 0, 1).reshape(128, 256))
    c["b2_row"] = np.ascontiguousarray(np.asarray(d["L0_ffn_b2"], f32).reshape(1, -1))
    c["esel"] = np.ascontiguousarray(np.repeat(np.eye(16, dtype=f32), 128, axis=1)).astype(bf)
    c["id128"] = np.eye(128, dtype=f32).astype(bf)
    c["ones_c"] = np.ones((128, 1), f32)
    c["ones_r"] = np.ones((1, 128), f32)
    c["ones_rb"] = np.ones((1, 128), f32).astype(bf)
    c["id128f"] = np.eye(128, dtype=f32)
    return c


def _core_maps(inputs):
    d = inputs
    x = np.asarray(d["L0_attn_x"], np.float32)
    common = _host_inputs(d)
    in_maps = []
    def _tile_x(xfm):
        return np.ascontiguousarray(
            xfm.reshape(8, 128, NT, TK).transpose(1, 2, 0, 3).reshape(128, NT * 8 * TK))

    for b in range(4):
        xT = np.ascontiguousarray(x[b].T)
        for half in range(2):
            m = dict(common)
            if half == 0:
                m["x"] = _tile_x(np.concatenate(
                    [np.zeros((D, OWN0), np.float32), xT[:, :OWN0]], axis=1))
                g = np.array([0, 0, 0, 0, 1, 1, 1, 1], np.float32)
                m["colg"] = np.zeros((DS, 1), np.float32)
            else:
                m["x"] = _tile_x(xT)
                g = np.ones(8, np.float32)
                m["colg"] = np.ones((DS, 1), np.float32)
            m["gxi"] = np.ascontiguousarray(np.broadcast_to(g, (128, 8)), np.float32)
            m["g16"] = np.ascontiguousarray(np.broadcast_to(g, (DS, 8)), np.float32)
            in_maps.append(m)
    return in_maps


def kernel(**inputs):
    nc = build_nc()
    in_maps = _core_maps(inputs)
    res = run_bass_kernel_spmd(nc, in_maps, core_ids=list(range(8)))
    out = np.empty((4, S, D), np.float32)
    for b in range(4):
        for half in range(2):
            out[b, half * OWN0:(half + 1) * OWN0] = res.results[b * 2 + half]["out"]
    return out
